# version 01
# baseline (speedup 1.0000x reference)
import numpy as np
import ml_dtypes
from contextlib import ExitStack
import concourse.bass as bass
import concourse.mybir as mybir
from concourse.bass_utils import run_bass_kernel_spmd

F32 = mybir.dt.float32
BF16 = mybir.dt.bfloat16
I32 = mybir.dt.int32
AF = mybir.ActivationFunctionType
ALU = mybir.AluOpType
AX = mybir.AxisListType
BF = ml_dtypes.bfloat16

B, T, D, H, DH, II = 8, 2048, 1024, 16, 64, 2816
NT, ND, NI = T // 128, D // 128, II // 128
SLOPES = [2.0 ** (-(h + 1) / 2.0) for h in range(H)]
EPS = 1e-5


class Sem:
    def __init__(self, nc, es, name):
        self.h = es.enter_context(nc.semaphore(name))
        self.n = 0

    def tick(self, instr, k=1):
        instr.then_inc(self.h, k)
        self.n += k
        return self.n


def make_consts():
    kbias = np.zeros((128, 384), np.float32)
    kk = np.arange(128, dtype=np.float32)
    for h in range(H):
        for kt in range(NT):
            kbias[:, h * 16 + kt] = SLOPES[h] * (kt * 128 + kk)
    kbias[:, 256:384] = 1.0
    q = np.arange(T, dtype=np.float32)
    qslope = np.stack([-SLOPES[h] * q for h in range(H)]).astype(BF)
    cbf = np.zeros((128, 256), np.float32)
    r = np.arange(128)
    cbf[:, 0:128] = np.where(r[:, None] <= r[None, :], 1e30, 0.0)
    cbf[:, 128:256] = 1.0
    return kbias, qslope, cbf.astype(BF)


def build_L1():
    nc = bass.Bass("TRN2", target_bir_lowering=False)
    x = nc.dram_tensor("x", [T, D], F32, kind="ExternalInput")
    mxx_o = nc.dram_tensor("mxx", [1, 1], F32, kind="ExternalOutput")
    fold = nc.dram_tensor("fold", [1, 128], F32, kind="Internal")
    es = ExitStack()
    with es:
        xb = [es.enter_context(nc.sbuf_tensor(f"xb{i}", [128, D], F32)) for i in range(3)]
        xnf = es.enter_context(nc.sbuf_tensor("xnf", [128, D], F32))
        sq = es.enter_context(nc.sbuf_tensor("sq", [128, D], F32))
        st = es.enter_context(nc.sbuf_tensor("st", [128, 64], F32))
        row = es.enter_context(nc.sbuf_tensor("row", [1, 128], F32))
        sx = [Sem(nc, es, f"sx{i}") for i in range(3)]
        sv = Sem(nc, es, "sv")
        sa = Sem(nc, es, "sa")
        vx = Sem(nc, es, "vx")
        sf = Sem(nc, es, "sf")
        so = Sem(nc, es, "so")
        block = es.enter_context(nc.Block())
        V0, A0 = sv.n, sa.n

        @block.sync
        def _(q):
            for tt in range(3):
                sx[tt].tick(q.dma_start(out=xb[tt][:],
                                        in_=x.ap()[tt * 128:(tt + 1) * 128, :]), 16)
            for tt in range(NT):
                b = tt % 3
                nxt = tt + 3
                if nxt < NT:
                    q.wait_ge(sv.h, V0 + 3 * tt + 3)
                    q.wait_ge(sa.h, A0 + 5 * tt + 1)
                    sx[b].tick(q.dma_start(out=xb[b][:],
                                           in_=x.ap()[nxt * 128:(nxt + 1) * 128, :]), 16)

        @block.vector
        def _(v):
            for tt in range(NT):
                b = tt % 3
                v.wait_ge(sx[b].h, 16 * (tt // 3 + 1))
                sv.tick(v.reduce_sum(out=st[:, 36:37], in_=xb[b][:], axis=AX.X))
                v.wait_ge(sa.h, A0 + 5 * tt + 3)
                n = vx.tick(v.tensor_tensor(out=st[:, 33:34], in0=st[:, 32:33],
                                            in1=st[:, 32:33], op=ALU.mult))
                v.wait_ge(vx.h, n)
                n = vx.tick(v.tensor_tensor(out=st[:, 33:34], in0=st[:, 34:35],
                                            in1=st[:, 33:34], op=ALU.subtract))
                v.wait_ge(vx.h, n)
                sv.tick(v.tensor_scalar(out=st[:, 33:34], in0=st[:, 33:34],
                                        scalar1=EPS, scalar2=None, op0=ALU.add))
                v.wait_ge(sa.h, A0 + 5 * tt + 5)
                n = vx.tick(v.tensor_scalar(out=xnf[:], in0=xb[b][:],
                                            scalar1=st[:, 32:33], scalar2=st[:, 35:36],
                                            op0=ALU.subtract, op1=ALU.mult))
                v.wait_ge(vx.h, n)
                sv.tick(v.reduce_max(out=st[:, tt:tt + 1], in_=xnf[:], axis=AX.X,
                                     apply_absolute_value=True))
            v.wait_ge(sv.h, V0 + 3 * NT)
            n = vx.tick(v.reduce_max(out=st[:, 16:17], in_=st[:, 0:16], axis=AX.X))
            v.wait_ge(vx.h, n)
            sv.tick(v.tensor_scalar(out=st[:, 16:17], in0=st[:, 16:17],
                                    scalar1=1.0, scalar2=None, op0=ALU.mult))

        @block.scalar
        def _(s):
            for tt in range(NT):
                b = tt % 3
                s.wait_ge(sx[b].h, 16 * (tt // 3 + 1))
                sa.tick(s.activation(out=sq[:], in_=xb[b][:], func=AF.Square,
                                     bias=0.0, scale=1.0, accum_out=st[:, 37:38]))
                s.wait_ge(sv.h, V0 + 3 * tt + 1)
                sa.tick(s.activation(out=st[:, 32:33], in_=st[:, 36:37],
                                     func=AF.Identity, bias=0.0, scale=1.0 / 1024.0))
                s.wait_ge(sa.h, A0 + 5 * tt + 2)
                sa.tick(s.activation(out=st[:, 34:35], in_=st[:, 37:38],
                                     func=AF.Identity, bias=0.0, scale=1.0 / 1024.0))
                s.wait_ge(sv.h, V0 + 3 * tt + 2)
                sa.tick(s.activation(out=st[:, 38:39], in_=st[:, 33:34],
                                     func=AF.Ln, bias=0.0, scale=1.0))
                s.wait_ge(sa.h, A0 + 5 * tt + 4)
                sa.tick(s.activation(out=st[:, 35:36], in_=st[:, 38:39],
                                     func=AF.Exp, bias=0.0, scale=-0.5))

        @block.sync
        def _(q, w=sv.n):
            q.wait_ge(sv.h, w)
            sf.tick(q.dma_start(out=fold.ap(), in_=st[:, 16:17],
                                allow_slow_non_contiguous=True), 16)
            q.wait_ge(sf.h, 16)
            sf.tick(q.dma_start(out=row[:], in_=fold.ap()), 16)
        @block.vector
        def _(v):
            v.wait_ge(sf.h, 32)
            sv.tick(v.reduce_max(out=row[0:1, 0:1], in_=row[0:1, 0:128], axis=AX.X))
        @block.scalar
        def _(s, w=sv.n):
            s.wait_ge(sv.h, w)
            so.tick(s.dma_start(out=mxx_o.ap(), in_=row[0:1, 0:1]), 16)
            s.wait_ge(so.h, 16)
    return nc


def build_L2():
    nc = bass.Bass("TRN2", target_bir_lowering=False)
    x = nc.dram_tensor("x", [T, D], F32, kind="ExternalInput")
    wqT = nc.dram_tensor("wqT", [D, D], F32, kind="ExternalInput")
    wkT = nc.dram_tensor("wkT", [D, D], F32, kind="ExternalInput")
    wvT = nc.dram_tensor("wvT", [D, D], F32, kind="ExternalInput")
    woT = nc.dram_tensor("woT", [D, D], F32, kind="ExternalInput")
    wg = nc.dram_tensor("wg", [II, D], F32, kind="ExternalInput")
    wu = nc.dram_tensor("wu", [II, D], F32, kind="ExternalInput")
    gmx8 = nc.dram_tensor("gmx8", [1, 8], F32, kind="ExternalInput")
    clipx = nc.dram_tensor("clipx", [1, 1], F32, kind="ExternalInput")
    kbias_d = nc.dram_tensor("kbias", [128, 384], F32, kind="ExternalInput")
    qslope_d = nc.dram_tensor("qslope", [16, T], BF16, kind="ExternalInput")
    cbf_d = nc.dram_tensor("cbf", [128, 256], BF16, kind="ExternalInput")

    acc_o = nc.dram_tensor("acc", [T, D], F32, kind="ExternalOutput")
    t_o = nc.dram_tensor("tmat", [II, T], F32, kind="ExternalOutput")
    mxt_o = nc.dram_tensor("mxt", [1, 1], F32, kind="ExternalOutput")

    xn_dr = nc.dram_tensor("xn_dr", [T, D], BF16, kind="Internal")
    kx_dr = nc.dram_tensor("kx_dr", [T, D], BF16, kind="Internal")
    kwg_dr = nc.dram_tensor("kwg_dr", [II, D], BF16, kind="Internal")
    kwu_dr = nc.dram_tensor("kwu_dr", [II, D], BF16, kind="Internal")
    fold = nc.dram_tensor("fold", [1, 128], F32, kind="Internal")

    es = ExitStack()
    with es:
        A1 = es.enter_context(nc.sbuf_tensor("A1", [128, 16384], BF16))
        A3 = es.enter_context(nc.sbuf_tensor("A3", [128, 16384], BF16))
        A4 = es.enter_context(nc.sbuf_tensor("A4", [128, 16384], BF16))
        A5 = es.enter_context(nc.sbuf_tensor("A5", [128, 16896], BF16))
        W1 = es.enter_context(nc.sbuf_tensor("W1", [128, 16384], BF16))
        R1 = es.enter_context(nc.sbuf_tensor("R1", [128, 6144], F32))
        SQ = es.enter_context(nc.sbuf_tensor("SQ", [128, 1024], F32))
        S1 = es.enter_context(nc.sbuf_tensor("S1", [128, 1408], F32))
        S2 = es.enter_context(nc.sbuf_tensor("S2", [128, 2560], BF16))
        CF = es.enter_context(nc.sbuf_tensor("CF", [128, 384], F32))
        CB = es.enter_context(nc.sbuf_tensor("CB", [128, 256], BF16))
        XI = es.enter_context(nc.sbuf_tensor("XI", [128, 1024], I32))

        pG0 = es.enter_context(nc.psum_tensor("pG0", [128, 512], F32))
        pG1 = es.enter_context(nc.psum_tensor("pG1", [128, 512], F32))
        pS0 = es.enter_context(nc.psum_tensor("pS0", [128, 512], F32))
        pS1 = es.enter_context(nc.psum_tensor("pS1", [128, 512], F32))
        pO = es.enter_context(nc.psum_tensor("pO", [128, 512], F32))
        pN = es.enter_context(nc.psum_tensor("pN", [128, 512], F32))

        M0, R0 = 0, 16
        SG, SU, SGX, SUX = 32, 54, 76, 98
        SXC, RSXC, TMX = 120, 121, 160
        LNT, SUMC, SQC, EX2, VARC = 135, 136, 137, 138, 139
        REC, SBB = 192, 704
        FR, FS = 1216, 1344

        s_ld = [Sem(nc, es, f"s_ld{i}") for i in range(3)]
        s_bn = [Sem(nc, es, f"s_bn{i}") for i in range(2)]
        s_w = [Sem(nc, es, f"s_w{i}") for i in range(3)]
        s_kw = [Sem(nc, es, f"s_kw{i}") for i in range(2)]
        s_qkv = [Sem(nc, es, f"s_qkv{i}") for i in range(2)]
        s_wo = Sem(nc, es, "s_wo")
        s_tr = Sem(nc, es, "s_tr")
        s_kwt = [Sem(nc, es, f"s_kwt{i}") for i in range(2)]
        s_tw = [Sem(nc, es, f"s_tw{i}") for i in range(2)]
        s_stg = [Sem(nc, es, f"s_stg{i}") for i in range(2)]
        s_odd = Sem(nc, es, "s_odd")
        s_xr = [Sem(nc, es, f"s_xr{i}") for i in range(2)]
        s_sc = Sem(nc, es, "s_sc")
        s_out = Sem(nc, es, "s_out")
        s_fld = Sem(nc, es, "s_fld")
        smT = Sem(nc, es, "smT")
        smV = Sem(nc, es, "smV")
        smS = Sem(nc, es, "smS")
        smM = Sem(nc, es, "smM")
        vx = Sem(nc, es, "vx")
        block = es.enter_context(nc.Block())

        @block.scalar
        def _(s):
            s_sc.tick(s.dma_start(out=CF[:], in_=kbias_d.ap()), 16)
            s_sc.tick(s.dma_start(out=CB[:], in_=cbf_d.ap()), 16)
            s_sc.tick(s.dma_start(out=S1[0:1, 150:158], in_=gmx8.ap()), 16)
            s_sc.tick(s.dma_start(out=S1[0:1, 158:159], in_=clipx.ap()), 16)
        @block.sync
        def _(q):
            for tt in range(3):
                s_ld[tt].tick(q.dma_start(out=R1[:, tt * 1024:(tt + 1) * 1024],
                                          in_=x.ap()[tt * 128:(tt + 1) * 128, :]), 16)
        @block.vector
        def _(v, w=s_sc.n):
            v.wait_ge(s_sc.h, w)
            smV.tick(v.reduce_max(out=S1[0:1, 159:160], in_=S1[0:1, 150:158], axis=AX.X))
        @block.scalar
        def _(s, w=smV.n):
            s.wait_ge(smV.h, w)
            smS.tick(s.activation(out=S1[0:1, 148:149], in_=S1[0:1, 158:159],
                                  func=AF.Exp, bias=0.0, scale=-1.0))
        @block.vector
        def _(v, w=smS.n):
            v.wait_ge(smS.h, w)
            n = vx.tick(v.tensor_scalar(out=S1[0:1, 148:149], in0=S1[0:1, 148:149],
                                        scalar1=1.0, scalar2=None, op0=ALU.add))
            v.wait_ge(vx.h, n)
            n = vx.tick(v.reciprocal(out=S1[0:1, 148:149], in_=S1[0:1, 148:149]))
            v.wait_ge(vx.h, n)
            n = vx.tick(v.tensor_tensor(out=S1[0:1, 147:148], in0=S1[0:1, 148:149],
                                        in1=S1[0:1, 159:160], op=ALU.mult))
            v.wait_ge(vx.h, n)
            n = vx.tick(v.tensor_scalar(out=S1[0:1, 147:148], in0=S1[0:1, 147:148],
                                        scalar1=1.0 / 127.0, scalar2=None, op0=ALU.mult))
            v.wait_ge(vx.h, n)
            smV.tick(v.reciprocal(out=S1[0:1, 146:147], in_=S1[0:1, 147:148]))
        @block.tensor
        def _(t, w=smV.n):
            t.wait_ge(smV.h, w)
            t.matmul(pN[:, 0:1], CF[0:1, 256:384], S1[0:1, 147:148], start=True, stop=True)
            smT.tick(t.matmul(pN[:, 1:2], CF[0:1, 256:384], S1[0:1, 146:147],
                              start=True, stop=True))
        @block.vector
        def _(v, w=smT.n):
            v.wait_ge(smT.h, w)
            v.tensor_copy(out=S1[:, SXC:SXC + 1], in_=pN[:, 0:1])
            smV.tick(v.tensor_copy(out=S1[:, RSXC:RSXC + 1], in_=pN[:, 1:2]))
        sxc_ready = smV.n

        V0, S0 = smV.n, smS.n
        @block.vector
        def _(v):
            for tt in range(NT):
                b, p = tt % 3, tt % 2
                v.wait_ge(s_ld[b].h, 16 * (tt // 3 + 1))
                v.wait_ge(s_bn[p].h, 32 * (tt // 2))
                if tt == 0:
                    v.wait_ge(smV.h, sxc_ready)
                xsl = R1[:, b * 1024:(b + 1) * 1024]
                wsl = W1[:, p * 2048:(p + 1) * 2048]
                smV.tick(v.reduce_sum(out=S1[:, SUMC:SUMC + 1], in_=xsl, axis=AX.X))
                v.wait_ge(smS.h, S0 + 5 * tt + 3)
                n = vx.tick(v.tensor_tensor(out=S1[:, VARC:VARC + 1],
                                            in0=S1[:, M0 + tt:M0 + tt + 1],
                                            in1=S1[:, M0 + tt:M0 + tt + 1], op=ALU.mult))
                v.wait_ge(vx.h, n)
                n = vx.tick(v.tensor_tensor(out=S1[:, VARC:VARC + 1],
                                            in0=S1[:, EX2:EX2 + 1],
                                            in1=S1[:, VARC:VARC + 1], op=ALU.subtract))
                v.wait_ge(vx.h, n)
                smV.tick(v.tensor_scalar(out=S1[:, VARC:VARC + 1], in0=S1[:, VARC:VARC + 1],
                                         scalar1=EPS, scalar2=None, op0=ALU.add))
                v.wait_ge(smS.h, S0 + 5 * tt + 5)
                smV.tick(v.tensor_scalar(out=wsl[:, 0:1024], in0=xsl,
                                         scalar1=S1[:, M0 + tt:M0 + tt + 1],
                                         scalar2=S1[:, R0 + tt:R0 + tt + 1],
                                         op0=ALU.subtract, op1=ALU.mult))
                n = vx.tick(v.tensor_tensor(out=S1[:, 143:144],
                                            in0=S1[:, R0 + tt:R0 + tt + 1],
                                            in1=S1[:, RSXC:RSXC + 1], op=ALU.mult))
                v.wait_ge(vx.h, n)
                n = vx.tick(v.tensor_scalar(out=XI[:, 0:1024], in0=xsl,
                                            scalar1=S1[:, M0 + tt:M0 + tt + 1],
                                            scalar2=S1[:, 143:144],
                                            op0=ALU.subtract, op1=ALU.mult))
                v.wait_ge(vx.h, n)
                smV.tick(v.tensor_scalar(out=wsl[:, 1024:2048], in0=XI[:, 0:1024],
                                         scalar1=127.0, scalar2=-128.0,
                                         op0=ALU.min, op1=ALU.max))
        @block.scalar
        def _(s):
            for tt in range(NT):
                b = tt % 3
                s.wait_ge(s_ld[b].h, 16 * (tt // 3 + 1))
                sa1 = smS.tick(s.activation(out=SQ[:], in_=R1[:, b * 1024:(b + 1) * 1024],
                                            func=AF.Square, bias=0.0, scale=1.0,
                                            accum_out=S1[:, SQC:SQC + 1]))
                s.wait_ge(smV.h, V0 + 4 * tt + 1)
                smS.tick(s.activation(out=S1[:, M0 + tt:M0 + tt + 1],
                                      in_=S1[:, SUMC:SUMC + 1], func=AF.Identity,
                                      bias=0.0, scale=1.0 / 1024.0))
                s.wait_ge(smS.h, S0 + 5 * tt + 2)
                smS.tick(s.activation(out=S1[:, EX2:EX2 + 1], in_=S1[:, SQC:SQC + 1],
                                      func=AF.Identity, bias=0.0, scale=1.0 / 1024.0))
                s.wait_ge(smV.h, V0 + 4 * tt + 2)
                smS.tick(s.activation(out=S1[:, LNT:LNT + 1], in_=S1[:, VARC:VARC + 1],
                                      func=AF.Ln, bias=0.0, scale=1.0))
                s.wait_ge(smS.h, S0 + 5 * tt + 4)
                smS.tick(s.activation(out=S1[:, R0 + tt:R0 + tt + 1],
                                      in_=S1[:, LNT:LNT + 1],
                                      func=AF.Exp, bias=0.0, scale=-0.5))
        @block.sync
        def _(q):
            for tt in range(NT):
                b, p = tt % 3, tt % 2
                q.wait_ge(smV.h, V0 + 4 * tt + 4)
                q.wait_ge(smS.h, S0 + 5 * tt + 1)
                wsl = W1[:, p * 2048:(p + 1) * 2048]
                s_bn[p].tick(q.dma_start(out=xn_dr.ap()[tt * 128:(tt + 1) * 128, :],
                                         in_=wsl[:, 0:1024]), 16)
                s_bn[p].tick(q.dma_start(out=kx_dr.ap()[tt * 128:(tt + 1) * 128, :],
                                         in_=wsl[:, 1024:2048]), 16)
                nxt = tt + 3
                if nxt < NT:
                    s_ld[b].tick(q.dma_start(out=R1[:, b * 1024:(b + 1) * 1024],
                                             in_=x.ap()[nxt * 128:(nxt + 1) * 128, :]), 16)
        p0_V = smV.n

        V1 = smV.n
        kwslot = [4096, 5120]
        jobs = [(wg, kwg_dr, SG), (wu, kwu_dr, SU)]
        @block.gpsimd
        def _(g):
            for j in range(2 * NI):
                wsrc = jobs[j // NI][0]
                it, b = j % NI, j % 3
                if j >= 3:
                    g.wait_ge(smV.h, V1 + (j - 3) + 1)
                s_w[b].tick(g.dma_start(out=R1[:, (3 + b) * 1024:(4 + b) * 1024],
                                        in_=wsrc.ap()[it * 128:(it + 1) * 128, :]), 16)
        @block.vector
        def _(v):
            for j in range(2 * NI):
                SGc = jobs[j // NI][2]
                it, b, p = j % NI, j % 3, j % 2
                v.wait_ge(s_w[b].h, 16 * (j // 3 + 1))
                v.wait_ge(s_kw[p].h, 16 * (j // 2))
                wsl = R1[:, (3 + b) * 1024:(4 + b) * 1024]
                n = vx.tick(v.reduce_max(out=S1[:, 142:143], in_=wsl, axis=AX.X,
                                         apply_absolute_value=True))
                v.wait_ge(vx.h, n)
                v.tensor_scalar(out=S1[:, SGc + it:SGc + it + 1], in0=S1[:, 142:143],
                                scalar1=1.0 / 127.0, scalar2=None, op0=ALU.mult)
                n = vx.tick(v.reciprocal(out=S1[:, 141:142], in_=S1[:, 142:143]))
                v.wait_ge(vx.h, n)
                n = vx.tick(v.tensor_scalar(out=S1[:, 141:142], in0=S1[:, 141:142],
                                            scalar1=127.0, scalar2=None, op0=ALU.mult))
                v.wait_ge(vx.h, n)
                n = vx.tick(v.tensor_scalar(out=XI[:, 0:1024], in0=wsl,
                                            scalar1=S1[:, 141:142], scalar2=None,
                                            op0=ALU.mult))
                v.wait_ge(vx.h, n)
                smV.tick(v.tensor_copy(out=W1[:, kwslot[p]:kwslot[p] + 1024],
                                       in_=XI[:, 0:1024]))
            for it in range(NI):
                v.tensor_tensor(out=S1[:, SGX + it:SGX + it + 1],
                                in0=S1[:, SG + it:SG + it + 1],
                                in1=S1[:, SXC:SXC + 1], op=ALU.mult)
                v.tensor_tensor(out=S1[:, SUX + it:SUX + it + 1],
                                in0=S1[:, SU + it:SU + it + 1],
                                in1=S1[:, SXC:SXC + 1], op=ALU.mult)
            smV.tick(v.memset(S1[:, TMX:TMX + NI], 0.0))
        @block.sync
        def _(q):
            for j in range(2 * NI):
                wdst = jobs[j // NI][1]
                it, p = j % NI, j % 2
                q.wait_ge(smV.h, V1 + j + 1)
                s_kw[p].tick(q.dma_start(out=wdst.ap()[it * 128:(it + 1) * 128, :],
                                         in_=W1[:, kwslot[p]:kwslot[p] + 1024]), 16)
        kw_all = (s_kw[0].n, s_kw[1].n)
        p1_V = smV.n

        xn_all = (s_bn[0].n, s_bn[1].n)
        @block.sync
        def _(q):
            q.wait_ge(s_bn[0].h, xn_all[0])
            q.wait_ge(s_bn[1].h, xn_all[1])
            for ft in range(ND):
                s_tr.tick(q.dma_start(out=A1[:, ft * 2048:(ft + 1) * 2048],
                                      in_=xn_dr.ap()[:, ft * 128:(ft + 1) * 128],
                                      transpose=True), 16)
        tr_all = s_tr.n

        @block.gpsimd
        def _(g):
            g.wait_ge(s_kw[0].h, kw_all[0])
            g.wait_ge(s_kw[1].h, kw_all[1])
            g.wait_ge(s_bn[0].h, xn_all[0])
            g.wait_ge(s_bn[1].h, xn_all[1])
            for kt in range(ND):
                s_qkv[0].tick(g.dma_start(out=W1[:, kt * 1024:(kt + 1) * 1024],
                                          in_=wqT.ap()[kt * 128:(kt + 1) * 128, :]), 16)
            for kt in range(ND):
                s_qkv[1].tick(g.dma_start(out=W1[:, 8192 + kt * 1024:8192 + (kt + 1) * 1024],
                                          in_=wkT.ap()[kt * 128:(kt + 1) * 128, :]), 16)

        @block.vector
        def _(v):
            for tt in range(NT):
                for h in range(H):
                    base = tt * 1056 + h * 66
                    smV.tick(v.memset(A5[:, base + 64:base + 66], 0.0))
            for tt in range(NT):
                for h in range(H):
                    base = tt * 1056 + h * 66
                    smV.tick(v.memset(A5[:, base + 64:base + 65], 1.0))
        vaug_init = smV.n

        bankfree = [(smV, 0), (smV, 0)]

        def emit_proj(kind, wbase, wsem, wval):
            groups = ([(m, c) for m in range(ND) for c in range(4)] if kind in "qk"
                      else [(m, c) for m in range(NT) for c in range(2)])
            mmvals = []
            @block.tensor
            def _(t):
                for gi, (m, c) in enumerate(groups):
                    bank = (pG0, pG1)[gi % 2]
                    if gi == 0:
                        t.wait_ge(wsem.h, wval)
                        t.wait_ge(s_tr.h, tr_all)
                    fs, fv = bankfree[gi % 2]
                    t.wait_ge(fs.h, fv)
                    ins = None
                    for kt in range(ND):
                        if kind in "qk":
                            lhs = W1[:, wbase + kt * 1024 + m * 128:
                                     wbase + kt * 1024 + m * 128 + 128]
                            rhs = A1[:, kt * 2048 + c * 512:kt * 2048 + (c + 1) * 512]
                        else:
                            lhs = A1[:, kt * 2048 + m * 128:kt * 2048 + m * 128 + 128]
                            rhs = W1[:, wbase + kt * 1024 + c * 512:
                                     wbase + kt * 1024 + (c + 1) * 512]
                        ins = t.matmul(bank[:], lhs, rhs,
                                       start=(kt == 0), stop=(kt == ND - 1))
                    mmvals.append(smT.tick(ins))
                    if kind == "q":
                        bankfree[gi % 2] = (smS, smS.n + gi + 1)
                    else:
                        bankfree[gi % 2] = (smV, smV.n + gi + 1)
            if kind == "q":
                @block.scalar
                def _(s):
                    for gi, (m, c) in enumerate(groups):
                        bank = (pG0, pG1)[gi % 2]
                        s.wait_ge(smT.h, mmvals[gi])
                        smS.tick(s.activation(
                            out=A3[:, m * 2048 + c * 512:m * 2048 + (c + 1) * 512],
                            in_=bank[:], func=AF.Identity, bias=0.0, scale=0.125))
            elif kind == "k":
                @block.vector
                def _(v):
                    for gi, (m, c) in enumerate(groups):
                        bank = (pG0, pG1)[gi % 2]
                        v.wait_ge(smT.h, mmvals[gi])
                        smV.tick(v.tensor_copy(
                            out=A4[:, m * 2048 + c * 512:m * 2048 + (c + 1) * 512],
                            in_=bank[:]))
            else:
                @block.vector
                def _(v):
                    for gi, (m, c) in enumerate(groups):
                        bank = (pG0, pG1)[gi % 2]
                        v.wait_ge(smT.h, mmvals[gi])
                        ins = None
                        for hh in range(8):
                            h = c * 8 + hh
                            dst = A5[:, m * 1056 + h * 66:m * 1056 + h * 66 + 64]
                            ins = v.tensor_copy(out=dst, in_=bank[:, hh * 64:(hh + 1) * 64])
                        smV.tick(ins)

        emit_proj("q", 0, s_qkv[0], s_qkv[0].n)
        q_evacs_done = smS.n
        @block.gpsimd
        def _(g):
            g.wait_ge(smS.h, q_evacs_done)
            for kt in range(ND):
                s_qkv[0].tick(g.dma_start(out=W1[:, kt * 1024:(kt + 1) * 1024],
                                          in_=wvT.ap()[kt * 128:(kt + 1) * 128, :]), 16)
        emit_proj("k", 8192, s_qkv[1], s_qkv[1].n)
        emit_proj("v", 0, s_qkv[0], s_qkv[0].n)
        p2_T, p2_V, p2_S = smT.n, smV.n, smS.n

        @block.sync
        def _(q):
            q.wait_ge(smT.h, p2_T)
            for ft in range(ND):
                s_tr.tick(q.dma_start(out=A1[:, ft * 2048:(ft + 1) * 2048],
                                      in_=kx_dr.ap()[:, ft * 128:(ft + 1) * 128],
                                      transpose=True), 16)
        tr2_all = s_tr.n

        CH = [(0, 512), (512, 512), (1024, 512), (1536, 512), (2048, 512), (2560, 256)]
        T4 = smT.n
        chunk_last_mm = []
        accv = T4
        for ci, (i0, ilen) in enumerate(CH):
            accv += 2 * (ilen // 128) * 4
            chunk_last_mm.append(accv)
        kwt_vals = [16 * 16 * (ci // 2 + 1) for ci in range(len(CH))]
        S4, V4 = p2_S, p2_V

        glist = []
        gidx = 0
        for ci, (i0, ilen) in enumerate(CH):
            for itl in range(ilen // 128):
                for c in range(4):
                    glist.append((gidx, ci, i0 // 128 + itl, c))
                    gidx += 1
        first_g_of_chunk = {}
        for g_, ci, it, c in glist:
            first_g_of_chunk.setdefault(ci, g_)

        def p4_chunk_load(q, ci):
            cb = ci % 2
            i0, ilen = CH[ci]
            gb, ub = cb * 8192, cb * 8192 + 4096
            if ci == 0:
                q.wait_ge(s_kw[0].h, kw_all[0])
                q.wait_ge(s_kw[1].h, kw_all[1])
                q.wait_ge(smT.h, p2_T)
            elif ci >= 2:
                q.wait_ge(smT.h, chunk_last_mm[ci - 2])
            for wdr, base in ((kwg_dr, gb), (kwu_dr, ub)):
                for ht in range(ND):
                    s_kwt[cb].tick(
                        q.dma_start(out=W1[:, base + ht * 512:base + ht * 512 + ilen],
                                    in_=wdr.ap()[i0:i0 + ilen, ht * 128:(ht + 1) * 128],
                                    transpose=True), 16)

        @block.sync
        def _(q):
            p4_chunk_load(q, 0)
            p4_chunk_load(q, 1)
            for g_, ci, it, c in glist:
                p = g_ % 2
                q.wait_ge(smV.h, V4 + 2 * g_ + 2)
                s_tw[p].tick(
                    q.dma_start(out=t_o.ap()[it * 128:(it + 1) * 128,
                                             c * 512:(c + 1) * 512],
                                in_=R1[:, 1024 + p * 512:1536 + p * 512]), 16)
                for ci2 in range(2, len(CH)):
                    if first_g_of_chunk[ci2] == g_ + 1:
                        p4_chunk_load(q, ci2)

        gmm, umm = [], []
        @block.tensor
        def _(t):
            for g_, ci, it, c in glist:
                cb = ci % 2
                gb, ub = cb * 8192, cb * 8192 + 4096
                i0, ilen = CH[ci]
                itl = it - i0 // 128
                gbank = (pG0, pS0)[g_ % 2]
                ubank = (pG1, pS1)[g_ % 2]
                t.wait_ge(s_kwt[cb].h, kwt_vals[ci])
                if g_ == 0:
                    t.wait_ge(s_tr.h, tr2_all)
                if g_ >= 2:
                    t.wait_ge(smS.h, S4 + g_ - 1)
                    t.wait_ge(smV.h, V4 + 2 * (g_ - 2) + 1)
                ins = None
                for ht in range(ND):
                    lhs = W1[:, gb + ht * 512 + itl * 128:gb + ht * 512 + itl * 128 + 128]
                    rhs = A1[:, ht * 2048 + c * 512:ht * 2048 + (c + 1) * 512]
                    ins = t.matmul(gbank[:], lhs, rhs, start=(ht == 0), stop=(ht == ND - 1))
                gmm.append(smT.tick(ins))
                ins = None
                for ht in range(ND):
                    lhs = W1[:, ub + ht * 512 + itl * 128:ub + ht * 512 + itl * 128 + 128]
                    rhs = A1[:, ht * 2048 + c * 512:ht * 2048 + (c + 1) * 512]
                    ins = t.matmul(ubank[:], lhs, rhs, start=(ht == 0), stop=(ht == ND - 1))
                umm.append(smT.tick(ins))

        @block.scalar
        def _(s):
            for g_, ci, it, c in glist:
                gbank = (pG0, pS0)[g_ % 2]
                s.wait_ge(smT.h, gmm[g_])
                if g_ >= 1:
                    s.wait_ge(smV.h, V4 + 2 * g_)
                smS.tick(s.activation(out=R1[:, 0:512], in_=gbank[:],
                                      func=AF.Silu, bias=0.0,
                                      scale=S1[:, SGX + it:SGX + it + 1]))

        @block.vector
        def _(v):
            for g_, ci, it, c in glist:
                p = g_ % 2
                ubank = (pG1, pS1)[g_ % 2]
                v.wait_ge(smT.h, umm[g_])
                uev = smV.tick(v.tensor_scalar(out=R1[:, 512:1024], in0=ubank[:],
                                               scalar1=S1[:, SUX + it:SUX + it + 1],
                                               scalar2=None, op0=ALU.mult))
                v.wait_ge(smS.h, S4 + g_ + 1)
                v.wait_ge(smV.h, uev)
                v.wait_ge(s_tw[p].h, 16 * (g_ // 2))
                n = vx.tick(v.tensor_tensor(out=R1[:, 1024 + p * 512:1536 + p * 512],
                                            in0=R1[:, 0:512], in1=R1[:, 512:1024],
                                            op=ALU.mult))
                v.wait_ge(vx.h, n)
                n = vx.tick(v.reduce_max(out=S1[:, 140:141],
                                         in_=R1[:, 1024 + p * 512:1536 + p * 512],
                                         axis=AX.X, apply_absolute_value=True))
                v.wait_ge(vx.h, n)
                smV.tick(v.tensor_tensor(out=S1[:, TMX + it:TMX + it + 1],
                                         in0=S1[:, TMX + it:TMX + it + 1],
                                         in1=S1[:, 140:141], op=ALU.max))
        p4_T, p4_V, p4_S = smT.n, smV.n, smS.n
        tw_all = (s_tw[0].n, s_tw[1].n)

        @block.gpsimd
        def _(g):
            g.wait_ge(smT.h, p4_T)
            for kt in range(ND):
                s_wo.tick(g.dma_start(out=W1[:, 8192 + kt * 1024:8192 + (kt + 1) * 1024],
                                      in_=woT.ap()[kt * 128:(kt + 1) * 128, :]), 16)
        @block.vector
        def _(v):
            v.wait_ge(smT.h, p4_T)
            v.memset(W1[64:65, 0:2048], 1.0)
            smV.tick(v.memset(W1[64:65, 2048:4096], 1.0))
        ones_init = smV.n

        headT_done = [p4_T] * H
        staged = [0] * H
        norm_read = smV.n

        def emit_stage(q, h):
            p = h % 2
            hf, hp = h // 2, h % 2
            if h >= 2:
                q.wait_ge(smT.h, headT_done[h - 2])
            else:
                q.wait_ge(smV.h, ones_init)
                q.wait_ge(smT.h, p4_T)
            s_stg[p].tick(q.dma_start(out=W1[0:64, p * 2048:(p + 1) * 2048],
                                      in_=A4[hp * 64:hp * 64 + 64,
                                             hf * 2048:(hf + 1) * 2048]), 16)
            s_stg[p].tick(q.dma_start(out=W1[0:64, 4096 + p * 2048:4096 + (p + 1) * 2048],
                                      in_=A3[hp * 64:hp * 64 + 64,
                                             hf * 2048:(hf + 1) * 2048]), 16)
            s_stg[p].tick(q.dma_start(out=W1[64:65, 4096 + p * 2048:4096 + (p + 1) * 2048],
                                      in_=qslope_d.ap()[h:h + 1, :]), 16)
            staged[h] = s_stg[p].n

        sbb_prev = smS.n
        odd_uses = 0
        for h in range(H):
            p = h % 2
            hf, hp = h // 2, h % 2
            kst, qst = p * 2048, 4096 + p * 2048
            if h < 2:
                @block.sync
                def _(q, h=h):
                    emit_stage(q, h)
            for qc in range(4):
                n_kt = min(NT, 4 * (qc + 1))
                diag = [kt for kt in range(n_kt) if kt >= 4 * qc]
                mmb = smT.n
                expb = smS.n
                mskb = smM.n
                pO_free = norm_read
                sbb_w = sbb_prev

                @block.tensor
                def _(t, h=h, qc=qc, n=n_kt, diag=diag, mmb=mmb, expb=expb, mskb=mskb,
                      kst=kst, qst=qst, pO_free=pO_free,
                      first=(h == 0 and qc == 0), p4S=p4_S):
                    state = {"d": 0}
                    def av(k2):
                        vs2 = max(0, k2 * 128 - qc * 512)
                        t.wait_ge(smS.h, expb + k2 + 1)
                        if k2 in diag:
                            state["d"] += 1
                            t.wait_ge(smM.h, mskb + state["d"])
                        ins = t.matmul(pO[0:65, vs2:512],
                                       A5[:, k2 * 1056 + h * 66:k2 * 1056 + h * 66 + 65],
                                       S2[:, (k2 % 3) * 512 + vs2:(k2 % 3) * 512 + 512],
                                       start=(k2 == 0), stop=(k2 == n - 1),
                                       skip_group_check=True)
                        smT.tick(ins)
                    for kt in range(n):
                        vs = max(0, kt * 128 - qc * 512)
                        if kt == 0:
                            t.wait_ge(s_stg[h % 2].h, staged[h])
                            t.wait_ge(smV.h, pO_free)
                            if first:
                                t.wait_ge(smS.h, p4S)
                        bank = pS0 if kt % 2 == 0 else pS1
                        ins = t.matmul(bank[:, vs:512],
                                       W1[0:65, kst + kt * 128:kst + kt * 128 + 128],
                                       W1[0:65, qst + qc * 512 + vs:qst + (qc + 1) * 512],
                                       start=True, stop=True)
                        smT.tick(ins)
                        if kt >= 1:
                            av(kt - 1)
                    av(n - 1)
                av_done = smT.n

                @block.scalar
                def _(s, h=h, qc=qc, n=n_kt, mmb=mmb):
                    for kt in range(n):
                        vs = max(0, kt * 128 - qc * 512)
                        bank = pS0 if kt % 2 == 0 else pS1
                        s.wait_ge(smT.h, mmb + (1 if kt == 0 else 2 * kt))
                        smS.tick(s.activation(
                            out=S2[:, (kt % 3) * 512 + vs:(kt % 3) * 512 + 512],
                            in_=bank[:, vs:512], func=AF.Exp,
                            bias=CF[:, h * 16 + kt:h * 16 + kt + 1], scale=1.0))
                @block.vector
                def _(v, qc=qc, diag=diag, expb=expb):
                    for kt in diag:
                        vs = max(0, kt * 128 - qc * 512)
                        v.wait_ge(smS.h, expb + kt + 1)
                        smM.tick(v.tensor_tensor(
                            out=S2[:, (kt % 3) * 512 + vs:(kt % 3) * 512 + vs + 128],
                            in0=S2[:, (kt % 3) * 512 + vs:(kt % 3) * 512 + vs + 128],
                            in1=CB[:, 0:128], op=ALU.min))

                @block.vector
                def _(v, av=av_done):
                    v.wait_ge(smT.h, av)
                    smV.tick(v.reciprocal(out=S1[64:65, REC:REC + 512],
                                          in_=pO[64:65, 0:512]))
                v1 = smV.n
                @block.tensor
                def _(t, v1=v1, sbb_w=sbb_w):
                    t.wait_ge(smV.h, v1)
                    t.wait_ge(smS.h, sbb_w)
                    smT.tick(t.matmul(pN[0:64, 0:512], CF[64:65, 256:320],
                                      S1[64:65, REC:REC + 512], start=True, stop=True))
                t1 = smT.n
                @block.scalar
                def _(s, t1=t1):
                    s.wait_ge(smT.h, t1)
                    smS.tick(s.activation(out=S1[0:64, SBB:SBB + 512], in_=pN[0:64, 0:512],
                                          func=AF.Identity, bias=0.0, scale=1.0))
                sbb_prev = smS.n
                s1v = smS.n
                if hp == 0:
                    dst = A1[0:64, hf * 2048 + qc * 512:hf * 2048 + (qc + 1) * 512]
                else:
                    dst = S2[0:64, 2048:2560]
                @block.vector
                def _(v, s1v=s1v, dst=dst, odd_uses=odd_uses, hp=hp):
                    v.wait_ge(smS.h, s1v)
                    if hp == 1:
                        v.wait_ge(s_odd.h, 16 * odd_uses)
                    smV.tick(v.tensor_tensor(out=dst, in0=pO[0:64, 0:512],
                                             in1=S1[0:64, SBB:SBB + 512], op=ALU.mult))
                norm_read = smV.n
                if hp == 1:
                    @block.sync
                    def _(q, w=norm_read, hf=hf, qc=qc):
                        q.wait_ge(smV.h, w)
                        s_odd.tick(q.dma_start(
                            out=A1[64:128, hf * 2048 + qc * 512:hf * 2048 + (qc + 1) * 512],
                            in_=S2[0:64, 2048:2560]), 16)
                    odd_uses += 1
            headT_done[h] = smT.n
            if h + 2 < H:
                @block.sync
                def _(q, h2=h + 2):
                    emit_stage(q, h2)
        p5_V = smV.n
        odd_all = s_odd.n

        @block.sync
        def _(q):
            q.wait_ge(smV.h, p4_V)
            for i in range(2):
                s_xr[i].tick(q.dma_start(out=R1[:, i * 1024:(i + 1) * 1024],
                                         in_=x.ap()[i * 128:(i + 1) * 128, :]), 16)
        o_mm = []
        @block.tensor
        def _(t):
            for m in range(NT):
                for c2 in range(2):
                    gi = m * 2 + c2
                    bank = (pG0, pG1)[gi % 2]
                    if gi == 0:
                        t.wait_ge(s_wo.h, s_wo.n)
                        t.wait_ge(smV.h, p5_V)
                        t.wait_ge(s_odd.h, odd_all)
                        t.wait_ge(smS.h, p4_S)
                    fs, fv = bankfree[gi % 2]
                    t.wait_ge(fs.h, fv)
                    ins = None
                    for kt in range(ND):
                        lhs = A1[:, kt * 2048 + m * 128:kt * 2048 + m * 128 + 128]
                        rhs = W1[:, 8192 + kt * 1024 + c2 * 512:
                                 8192 + kt * 1024 + (c2 + 1) * 512]
                        ins = t.matmul(bank[:], lhs, rhs,
                                       start=(kt == 0), stop=(kt == ND - 1))
                    o_mm.append(smT.tick(ins))
                    bankfree[gi % 2] = (smV, smV.n + gi + 1)
        V6 = smV.n
        @block.vector
        def _(v):
            for m in range(NT):
                px, pa = m % 2, m % 2
                for c2 in range(2):
                    gi = m * 2 + c2
                    bank = (pG0, pG1)[gi % 2]
                    v.wait_ge(smT.h, o_mm[gi])
                    if gi == 0:
                        v.wait_ge(s_tw[0].h, tw_all[0])
                        v.wait_ge(s_tw[1].h, tw_all[1])
                    v.wait_ge(s_xr[px].h, 16 * (m // 2 + 1))
                    if c2 == 0 and m >= 2:
                        v.wait_ge(s_out.h, 16 * (m - 1))
                    smV.tick(v.tensor_add(
                        out=R1[:, 2048 + pa * 1024 + c2 * 512:
                               2048 + pa * 1024 + (c2 + 1) * 512],
                        in0=bank[:],
                        in1=R1[:, px * 1024 + c2 * 512:px * 1024 + (c2 + 1) * 512]))
        @block.sync
        def _(q):
            for m in range(NT):
                px, pa = m % 2, m % 2
                q.wait_ge(smV.h, V6 + 2 * m + 2)
                s_out.tick(q.dma_start(out=acc_o.ap()[m * 128:(m + 1) * 128, :],
                                       in_=R1[:, 2048 + pa * 1024:2048 + (pa + 1) * 1024]), 16)
                nxt = m + 2
                if nxt < NT:
                    s_xr[px].tick(q.dma_start(out=R1[:, px * 1024:(px + 1) * 1024],
                                              in_=x.ap()[nxt * 128:(nxt + 1) * 128, :]), 16)

        @block.vector
        def _(v, w=p4_V):
            v.wait_ge(smV.h, w)
            smV.tick(v.reduce_max(out=S1[:, 140:141], in_=S1[:, TMX:TMX + NI], axis=AX.X))
        @block.sync
        def _(q, w=smV.n):
            q.wait_ge(smV.h, w)
            s_fld.tick(q.dma_start(out=fold.ap(), in_=S1[:, 140:141],
                                   allow_slow_non_contiguous=True), 16)
            q.wait_ge(s_fld.h, 16)
            s_fld.tick(q.dma_start(out=S1[0:1, FR:FR + 128], in_=fold.ap()), 16)
        @block.vector
        def _(v, w=s_fld.n):
            v.wait_ge(s_fld.h, w)
            smV.tick(v.reduce_max(out=S1[0:1, FS:FS + 1], in_=S1[0:1, FR:FR + 128],
                                  axis=AX.X))
        @block.scalar
        def _(s, w=smV.n):
            s.wait_ge(smV.h, w)
            s_out.tick(s.dma_start(out=mxt_o.ap(), in_=S1[0:1, FS:FS + 1]), 16)
        @block.sync
        def _(q):
            q.wait_ge(s_out.h, s_out.n)
            q.wait_ge(s_tw[0].h, tw_all[0])
            q.wait_ge(s_tw[1].h, tw_all[1])
    return nc


def build_L3():
    nc = bass.Bass("TRN2", target_bir_lowering=False)
    acc = nc.dram_tensor("acc", [T, D], F32, kind="ExternalInput")
    tmat = nc.dram_tensor("tmat", [II, T], F32, kind="ExternalInput")
    wd = nc.dram_tensor("wd", [D, II], F32, kind="ExternalInput")
    gmt8 = nc.dram_tensor("gmt8", [1, 8], F32, kind="ExternalInput")
    clipd = nc.dram_tensor("clipd", [1, 1], F32, kind="ExternalInput")
    kbias_d = nc.dram_tensor("kbias", [128, 384], F32, kind="ExternalInput")
    out_o = nc.dram_tensor("out", [T, D], F32, kind="ExternalOutput")
    wqd_dr = nc.dram_tensor("wqd_dr", [D, II], BF16, kind="Internal")

    es = ExitStack()
    with es:
        KH = es.enter_context(nc.sbuf_tensor("KH", [128, NI * 2048], BF16))
        WD = es.enter_context(nc.sbuf_tensor("WD", [128, NI * 1024], BF16))
        RT = es.enter_context(nc.sbuf_tensor("RT", [128, 4096], F32))
        RA = es.enter_context(nc.sbuf_tensor("RA", [128, 4096], F32))
        WQ = es.enter_context(nc.sbuf_tensor("WQ", [128, 2816], F32))
        SB = es.enter_context(nc.sbuf_tensor("SB", [128, 2816], BF16))
        XI = es.enter_context(nc.sbuf_tensor("XI", [128, 2816], I32))
        S1 = es.enter_context(nc.sbuf_tensor("S1", [128, 64], F32))
        CF = es.enter_context(nc.sbuf_tensor("CF", [128, 384], F32))
        pD0 = es.enter_context(nc.psum_tensor("pD0", [128, 512], F32))
        pD1 = es.enter_context(nc.psum_tensor("pD1", [128, 512], F32))
        pN = es.enter_context(nc.psum_tensor("pN", [128, 512], F32))

        s_sc = Sem(nc, es, "s_sc")
        s_wq = Sem(nc, es, "s_wq")
        s_sb = Sem(nc, es, "s_sb")
        s_wt = Sem(nc, es, "s_wt")
        s_tl = [Sem(nc, es, f"s_tl{i}") for i in range(2)]
        s_ar = [Sem(nc, es, f"s_ar{i}") for i in range(2)]
        s_out = Sem(nc, es, "s_out")
        smT = Sem(nc, es, "smT")
        smV = Sem(nc, es, "smV")
        smS = Sem(nc, es, "smS")
        vx = Sem(nc, es, "vx")
        block = es.enter_context(nc.Block())

        @block.scalar
        def _(s):
            s_sc.tick(s.dma_start(out=CF[:], in_=kbias_d.ap()), 16)
            s_sc.tick(s.dma_start(out=S1[0:1, 32:40], in_=gmt8.ap()), 16)
            s_sc.tick(s.dma_start(out=S1[0:1, 40:41], in_=clipd.ap()), 16)
        @block.vector
        def _(v, w=s_sc.n):
            v.wait_ge(s_sc.h, w)
            smV.tick(v.reduce_max(out=S1[0:1, 41:42], in_=S1[0:1, 32:40], axis=AX.X))
        @block.scalar
        def _(s, w=smV.n):
            s.wait_ge(smV.h, w)
            smS.tick(s.activation(out=S1[0:1, 42:43], in_=S1[0:1, 40:41],
                                  func=AF.Exp, bias=0.0, scale=-1.0))
        @block.vector
        def _(v, w=smS.n):
            v.wait_ge(smS.h, w)
            n = vx.tick(v.tensor_scalar(out=S1[0:1, 42:43], in0=S1[0:1, 42:43],
                                        scalar1=1.0, scalar2=None, op0=ALU.add))
            v.wait_ge(vx.h, n)
            n = vx.tick(v.reciprocal(out=S1[0:1, 42:43], in_=S1[0:1, 42:43]))
            v.wait_ge(vx.h, n)
            n = vx.tick(v.tensor_tensor(out=S1[0:1, 43:44], in0=S1[0:1, 42:43],
                                        in1=S1[0:1, 41:42], op=ALU.mult))
            v.wait_ge(vx.h, n)
            n = vx.tick(v.tensor_scalar(out=S1[0:1, 43:44], in0=S1[0:1, 43:44],
                                        scalar1=1.0 / 127.0, scalar2=None, op0=ALU.mult))
            v.wait_ge(vx.h, n)
            smV.tick(v.reciprocal(out=S1[0:1, 44:45], in_=S1[0:1, 43:44]))
        @block.tensor
        def _(t, w=smV.n, w2=s_sc.n):
            t.wait_ge(smV.h, w)
            t.wait_ge(s_sc.h, w2)
            t.matmul(pN[:, 0:1], CF[0:1, 256:384], S1[0:1, 43:44], start=True, stop=True)
            smT.tick(t.matmul(pN[:, 1:2], CF[0:1, 256:384], S1[0:1, 44:45],
                              start=True, stop=True))
        @block.vector
        def _(v, w=smT.n):
            v.wait_ge(smT.h, w)
            v.tensor_copy(out=S1[:, 0:1], in_=pN[:, 0:1])
            smV.tick(v.tensor_copy(out=S1[:, 1:2], in_=pN[:, 1:2]))
        shc = smV.n

        V1 = smV.n
        @block.gpsimd
        def _(g):
            for pt in range(ND):
                if pt >= 1:
                    g.wait_ge(smV.h, V1 + pt)
                s_wq.tick(g.dma_start(out=WQ[:], in_=wd.ap()[pt * 128:(pt + 1) * 128, :]), 16)
        @block.vector
        def _(v):
            for pt in range(ND):
                v.wait_ge(s_wq.h, 16 * (pt + 1))
                if pt >= 1:
                    v.wait_ge(s_sb.h, 16 * pt)
                n = vx.tick(v.reduce_max(out=S1[:, 8:9], in_=WQ[:], axis=AX.X,
                                         apply_absolute_value=True))
                v.wait_ge(vx.h, n)
                v.tensor_scalar(out=S1[:, 9:10], in0=S1[:, 8:9], scalar1=1.0 / 127.0,
                                scalar2=None, op0=ALU.mult)
                n = vx.tick(v.reciprocal(out=S1[:, 10:11], in_=S1[:, 8:9]))
                v.wait_ge(vx.h, n)
                n = vx.tick(v.tensor_scalar(out=S1[:, 10:11], in0=S1[:, 10:11],
                                            scalar1=127.0, scalar2=None, op0=ALU.mult))
                v.wait_ge(vx.h, n)
                n = vx.tick(v.tensor_scalar(out=XI[:], in0=WQ[:], scalar1=S1[:, 10:11],
                                            scalar2=None, op0=ALU.mult))
                v.wait_ge(vx.h, n)
                smV.tick(v.tensor_scalar(out=SB[:], in0=XI[:], scalar1=S1[:, 9:10],
                                         scalar2=None, op0=ALU.mult))
        @block.sync
        def _(q):
            for pt in range(ND):
                q.wait_ge(smV.h, V1 + pt + 1)
                s_sb.tick(q.dma_start(out=wqd_dr.ap()[pt * 128:(pt + 1) * 128, :],
                                      in_=SB[:]), 16)
        @block.sync
        def _(q, w=s_sb.n):
            q.wait_ge(s_sb.h, w)
            for it in range(NI):
                s_wt.tick(q.dma_start(out=WD[:, it * 1024:(it + 1) * 1024],
                                      in_=wqd_dr.ap()[:, it * 128:(it + 1) * 128],
                                      transpose=True), 16)
        wt_all = s_wt.n

        V2 = smV.n
        @block.sync
        def _(q):
            for it in range(NI):
                p = it % 2
                if it >= 2:
                    q.wait_ge(smV.h, V2 + it - 1)
                s_tl[p].tick(q.dma_start(out=RT[:, p * 2048:(p + 1) * 2048],
                                         in_=tmat.ap()[it * 128:(it + 1) * 128, :]), 16)
        @block.vector
        def _(v):
            for it in range(NI):
                p = it % 2
                v.wait_ge(s_tl[p].h, 16 * (it // 2 + 1))
                if it == 0:
                    v.wait_ge(smV.h, shc)
                n = vx.tick(v.tensor_scalar(out=XI[:, 0:2048],
                                            in0=RT[:, p * 2048:(p + 1) * 2048],
                                            scalar1=S1[:, 1:2], scalar2=None,
                                            op0=ALU.mult))
                v.wait_ge(vx.h, n)
                smV.tick(v.tensor_scalar(out=KH[:, it * 2048:(it + 1) * 2048],
                                         in0=XI[:, 0:2048], scalar1=127.0, scalar2=-128.0,
                                         op0=ALU.min, op1=ALU.max))
        kh_all = smV.n

        d_mm = []
        @block.tensor
        def _(t):
            for m in range(NT):
                for hc in range(2):
                    gi = m * 2 + hc
                    bank = (pD0, pD1)[gi % 2]
                    if gi == 0:
                        t.wait_ge(smV.h, kh_all)
                        t.wait_ge(s_wt.h, wt_all)
                    if gi >= 2:
                        t.wait_ge(smV.h, kh_all + gi - 1)
                    ins = None
                    for it in range(NI):
                        lhs = KH[:, it * 2048 + m * 128:it * 2048 + m * 128 + 128]
                        rhs = WD[:, it * 1024 + hc * 512:it * 1024 + (hc + 1) * 512]
                        ins = t.matmul(bank[:], lhs, rhs,
                                       start=(it == 0), stop=(it == NI - 1))
                    d_mm.append(smT.tick(ins))
        @block.sync
        def _(q):
            for i in range(2):
                s_ar[i].tick(q.dma_start(out=RA[:, i * 1024:(i + 1) * 1024],
                                         in_=acc.ap()[i * 128:(i + 1) * 128, :]), 16)
        V3 = smV.n
        @block.vector
        def _(v):
            for m in range(NT):
                pa = m % 2
                for hc in range(2):
                    gi = m * 2 + hc
                    bank = (pD0, pD1)[gi % 2]
                    v.wait_ge(smT.h, d_mm[gi])
                    v.wait_ge(s_ar[pa].h, 16 * (m // 2 + 1))
                    if hc == 0 and m >= 2:
                        v.wait_ge(s_out.h, 16 * (m - 1))
                    smV.tick(v.scalar_tensor_tensor(
                        out=RA[:, 2048 + pa * 1024 + hc * 512:
                                2048 + pa * 1024 + (hc + 1) * 512],
                        in0=bank[:], scalar=S1[:, 0:1],
                        in1=RA[:, pa * 1024 + hc * 512:pa * 1024 + (hc + 1) * 512],
                        op0=ALU.mult, op1=ALU.add))
        @block.sync
        def _(q):
            for m in range(NT):
                pa = m % 2
                q.wait_ge(smV.h, V3 + 2 * m + 2)
                s_out.tick(q.dma_start(out=out_o.ap()[m * 128:(m + 1) * 128, :],
                                       in_=RA[:, 2048 + pa * 1024:2048 + (pa + 1) * 1024]), 16)
                nxt = m + 2
                if nxt < NT:
                    s_ar[pa].tick(q.dma_start(out=RA[:, pa * 1024:(pa + 1) * 1024],
                                              in_=acc.ap()[nxt * 128:(nxt + 1) * 128, :]), 16)
            q.wait_ge(s_out.h, s_out.n)
    return nc


_CACHE = {}


def kernel(L0_attn_x, L0_attn_Wq, L0_attn_Wk, L0_attn_Wv, L0_attn_Wo,
           L0_ffn_clip_x, L0_ffn_clip_down, L0_ffn_w_gate, L0_ffn_w_up,
           L0_ffn_w_down):
    x = np.ascontiguousarray(np.asarray(L0_attn_x, np.float32))
    wqT = np.ascontiguousarray(np.asarray(L0_attn_Wq, np.float32).T)
    wkT = np.ascontiguousarray(np.asarray(L0_attn_Wk, np.float32).T)
    wvT = np.ascontiguousarray(np.asarray(L0_attn_Wv, np.float32).T)
    woT = np.ascontiguousarray(np.asarray(L0_attn_Wo, np.float32).T)
    wg = np.ascontiguousarray(np.asarray(L0_ffn_w_gate, np.float32))
    wu = np.ascontiguousarray(np.asarray(L0_ffn_w_up, np.float32))
    wd = np.ascontiguousarray(np.asarray(L0_ffn_w_down, np.float32))
    clipx = np.asarray(L0_ffn_clip_x, np.float32).reshape(1, 1)
    clipd = np.asarray(L0_ffn_clip_down, np.float32).reshape(1, 1)
    kbias, qslope, cbf = make_consts()

    if "L1" not in _CACHE:
        _CACHE["L1"] = build_L1()
        _CACHE["L2"] = build_L2()
        _CACHE["L3"] = build_L3()

    cores = list(range(B))
    maps1 = [{"x": np.ascontiguousarray(x[b])} for b in range(B)]
    r1 = run_bass_kernel_spmd(_CACHE["L1"], maps1, core_ids=cores)
    gmx8 = np.concatenate([np.asarray(r1.results[b]["mxx"]).reshape(1, 1)
                           for b in range(B)], axis=1)

    maps2 = [{"x": np.ascontiguousarray(x[b]), "wqT": wqT, "wkT": wkT,
              "wvT": wvT, "woT": woT, "wg": wg, "wu": wu, "gmx8": gmx8,
              "clipx": clipx, "kbias": kbias, "qslope": qslope, "cbf": cbf}
             for b in range(B)]
    r2 = run_bass_kernel_spmd(_CACHE["L2"], maps2, core_ids=cores)
    gmt8 = np.concatenate([np.asarray(r2.results[b]["mxt"]).reshape(1, 1)
                           for b in range(B)], axis=1)

    maps3 = [{"acc": np.asarray(r2.results[b]["acc"]),
              "tmat": np.asarray(r2.results[b]["tmat"]),
              "wd": wd, "gmt8": gmt8, "clipd": clipd, "kbias": kbias}
             for b in range(B)]
    r3 = run_bass_kernel_spmd(_CACHE["L3"], maps3, core_ids=cores)
    out = np.stack([np.asarray(r3.results[b]["out"]) for b in range(B)], axis=0)
    return out.astype(np.float32)
